# version 02
# baseline (speedup 1.0000x reference)
import numpy as np
import ml_dtypes
import concourse.bass as bass
import concourse.mybir as mybir
import bass_rust as _bass_rust
from concourse.bass_utils import run_bass_kernel_spmd
from concourse.library_config import all_libraries, standard
from concourse.alu_op_type import AluOpType as OP
from contextlib import ExitStack

F32 = mybir.dt.float32
F16 = mybir.dt.float16
BF16 = mybir.dt.bfloat16
AF = mybir.ActivationFunctionType
AX = mybir.AxisListType

import os
KSTOP = os.environ.get("KSTOP", "full")

MAGIC = float(np.float32(1.5 * 2.0 ** 23))
EPS = 1e-5
NB = 2
NT = 16
NU = 64


def build():
    nc = bass.Bass("TRN2", target_bir_lowering=False, debug=False)

    x_in = nc.dram_tensor("x_in", [NB * 2048, 1024], F32, kind="ExternalInput").ap()
    wq_d = nc.dram_tensor("wq", [1024, 1024], F32, kind="ExternalInput").ap()
    wk_d = nc.dram_tensor("wk", [1024, 1024], F32, kind="ExternalInput").ap()
    wv_d = nc.dram_tensor("wv", [1024, 1024], F32, kind="ExternalInput").ap()
    bq_d = nc.dram_tensor("bq", [1024], F32, kind="ExternalInput").ap()
    bk_d = nc.dram_tensor("bk", [1024], F32, kind="ExternalInput").ap()
    bv_d = nc.dram_tensor("bv", [1024], F32, kind="ExternalInput").ap()
    wf_d = nc.dram_tensor("wf", [1024, 1024], F32, kind="ExternalInput").ap()
    gam_d = nc.dram_tensor("gam", [1024], F32, kind="ExternalInput").ap()
    bet_d = nc.dram_tensor("bet", [1024], F32, kind="ExternalInput").ap()
    id16_d = nc.dram_tensor("id16", [128, 128], BF16, kind="ExternalInput").ap()
    idf_d = nc.dram_tensor("idf", [128, 128], F32, kind="ExternalInput").ap()
    out_d = nc.dram_tensor("out", [NB * 2048, 1024], F32, kind="ExternalOutput").ap()
    x1sp_d = nc.dram_tensor("x1sp", [NB * 2048, 1024], F16, kind="Internal").ap()
    xnsp_d = nc.dram_tensor("xnsp", [NB * 2048, 1024], F16, kind="Internal").ap()
    tern_d = nc.dram_tensor("ternsp", [1024, 1024], BF16, kind="Internal").ap()

    sem_names = [
        "dC", "dB", "dRow", "dXB0", "dXB1", "dW", "dXF0", "dXF1", "dS1",
        "dS20", "dS21", "dXNa0", "dXNa1", "dXNb0", "dXNb1", "dOU0", "dOU1",
        "dRL", "dTN0", "dTN1", "dT5",
        "vP0", "tP0", "sTN", "tTN", "vTN",
        "t_p1", "s_p1",
        "t_qkv", "s_qt", "s_kt", "v_vev",
        "t_sc", "s_exp", "v_z", "s_inv", "v_r", "t_pv", "s_ctx",
        "t_p4", "v_p4a", "v_p4b", "s_p4b", "v_p4b2", "s_ln1", "v_p4c",
        "v_p4d", "s_p4d", "v_p4d2", "s_ln2", "v_p4e",
        "t_sy", "v_sy", "v_pr", "r_prep", "v_gm",
        "rs0", "rs1", "rs2",
        "v_q", "t_tr5", "s_xq", "t_ffn", "v_ffn",
        "v_p5b", "s_p5b", "v_p5b2", "s_ln3", "vch", "sch",
    ]
    cnt = {n: 0 for n in sem_names}

    with ExitStack() as top:
        e = top.enter_context

        id16 = e(nc.sbuf_tensor("id16s", [128, 128], BF16))
        idf = e(nc.sbuf_tensor("idfs", [128, 128], F32))
        ones_c = e(nc.sbuf_tensor("ones_c", [128, 1], F32))
        ones_r = e(nc.sbuf_tensor("ones_r", [1, 128], F32))
        bqt = e(nc.sbuf_tensor("bqt", [128, 8], F32))
        bkt = e(nc.sbuf_tensor("bkt", [128, 8], F32))
        scal = e(nc.sbuf_tensor("scal", [128, 16], F32))
        inbox = e(nc.sbuf_tensor("inbox", [128, 4], F32))
        mx1 = e(nc.sbuf_tensor("mx1", [128, 1], F32))

        S = {n: e(nc.semaphore(n)) for n in sem_names}
        _block = e(nc.Block())
        _stop = {"on": False, "waits": None}

        class _GBlock:
            def __getattr__(self, eng):
                def add(f):
                    if not _stop["on"]:
                        getattr(_block, eng)(f)
                return add
        block = _GBlock()

        def ckpt(name, waits):
            if KSTOP == name and not _stop["on"]:
                _stop["on"] = True
                _stop["waits"] = list(waits)

        def wg(eng, sem, v):
            if v > 0:
                eng.wait_ge(S[sem], v)

        def winc(inst, sem, by=1):
            cnt[sem] += by
            inst.then_inc(S[sem], by)
            return cnt[sem]

        p0 = ExitStack()
        wfull = p0.enter_context(nc.sbuf_tensor("wfull", [128, 8192], F32))
        aws = p0.enter_context(nc.sbuf_tensor("aws", [128, 8], F32))
        asum = p0.enter_context(nc.sbuf_tensor("asum", [128, 1], F32))
        npos = p0.enter_context(nc.sbuf_tensor("npos", [128, 8], F32))
        nneg = p0.enter_context(nc.sbuf_tensor("nneg", [128, 8], F32))
        s1c = p0.enter_context(nc.sbuf_tensor("s1c", [128, 8], F32))
        nsum = p0.enter_context(nc.sbuf_tensor("nsum", [128, 1], F32))
        s1sum = p0.enter_context(nc.sbuf_tensor("s1sum", [128, 1], F32))
        mp0 = p0.enter_context(nc.sbuf_tensor("mp0", [128, 1024], F32))
        mn0 = p0.enter_context(nc.sbuf_tensor("mn0", [128, 1024], F32))
        ternf = p0.enter_context(nc.sbuf_tensor("ternf", [128, 1024], F32))
        tnat = p0.enter_context(nc.sbuf_tensor("tnat", [128, 2048], BF16))
        ttev = p0.enter_context(nc.sbuf_tensor("ttev", [128, 2048], BF16))
        delta_b = p0.enter_context(nc.sbuf_tensor("delta_b", [128, 1], F32))
        ndelta_b = p0.enter_context(nc.sbuf_tensor("ndelta_b", [128, 1], F32))
        ps0 = p0.enter_context(nc.psum_tensor("ps0", [128, 512], F32))
        pstA = p0.enter_context(nc.psum_tensor("pstA", [128, 512], F32))
        pstB = p0.enter_context(nc.psum_tensor("pstB", [128, 512], F32))

        def b_sync(sync):
            for dst, src in [
                (id16[:], id16_d), (idf[:], idf_d),
                (wfull[:], wf_d.rearrange("(r p) n -> p r n", p=128)),
            ]:
                winc(sync.dma_start(out=dst, in_=src), "dC", 16)
            for dst, src in [
                (bqt[:], bq_d.rearrange("(g p) -> p g", p=128)),
                (bkt[:], bk_d.rearrange("(g p) -> p g", p=128)),
            ]:
                winc(sync.dma_start(out=dst, in_=src,
                                    allow_slow_non_contiguous=True), "dC", 16)
        block.sync(b_sync)
        dC_all = cnt["dC"]

        def b_vec(vector):
            vector.memset(ones_c[:], 1.0)
            vector.memset(ones_r[:], 1.0)
            vector.memset(scal[:], 0.0)
            vector.memset(inbox[:], 0.0)
            vector.wait_ge(S["dC"], dC_all)
            for r in range(8):
                red = vector.tensor_reduce(aws[:, r:r + 1], wfull[:, r * 1024:(r + 1) * 1024],
                                           AX.X, OP.add, apply_absolute_value=True)
            vector.drain()
            winc(vector.tensor_reduce(asum[:], aws[:], AX.X, OP.add), "vP0")
        block.vector(b_vec)
        vP0_asum = cnt["vP0"]

        def b_tens(tensor):
            tensor.wait_ge(S["vP0"], vP0_asum)
            winc(tensor.matmul(ps0[0:1, 0:1], ones_c[:], asum[:], start=True, stop=True),
                 "tP0")
        block.tensor(b_tens)
        tP0_asum = cnt["tP0"]

        def b_vec2(vector):
            vector.wait_ge(S["tP0"], tP0_asum)
            winc(vector.tensor_scalar(scal[0:1, 5:6], ps0[0:1, 0:1],
                                      float(0.7 / (1024.0 * 1024.0)), None, OP.mult),
                 "vP0")
        block.vector(b_vec2)
        vP0_delta = cnt["vP0"]

        def b_tens_d(tensor):
            tensor.wait_ge(S["vP0"], vP0_delta)
            winc(tensor.matmul(ps0[:, 4:5], ones_r[:], scal[0:1, 5:6],
                               start=True, stop=True), "tP0")
        block.tensor(b_tens_d)
        tP0_delta = cnt["tP0"]

        def b_vec3(vector):
            vector.wait_ge(S["tP0"], tP0_delta)
            vector.tensor_copy(delta_b[:], ps0[:, 4:5])
            vector.tensor_scalar(ndelta_b[:], ps0[:, 4:5], -1.0, None, OP.mult)
            winc(vector.drain(), "vP0")
            for r in range(8):
                wcol = wfull[:, r * 1024:(r + 1) * 1024]
                wg(vector, "tTN", 2 * r - 2)
                vector.tensor_scalar(mp0[:], wcol, delta_b[:], 0.0, OP.is_gt, OP.add,
                                     accum_out=npos[:, r:r + 1])
                vector.tensor_scalar(mn0[:], wcol, ndelta_b[:], 0.0, OP.is_lt, OP.add,
                                     accum_out=nneg[:, r:r + 1])
                vector.drain()
                vector.tensor_tensor(ternf[:], mp0[:], mn0[:], OP.subtract)
                vector.drain()
                vector.scalar_tensor_tensor(mp0[:], wcol, 1.0, ternf[:], OP.mult,
                                            OP.mult, accum_out=s1c[:, r:r + 1])
                winc(vector.tensor_copy(tnat[:, (r % 2) * 1024:(r % 2 + 1) * 1024],
                                        ternf[:]), "vTN")
                vector.drain()
            vector.tensor_tensor(npos[:], npos[:], nneg[:], OP.add)
            vector.drain()
            vector.tensor_reduce(nsum[:], npos[:], AX.X, OP.add)
            winc(vector.tensor_reduce(s1sum[:], s1c[:], AX.X, OP.add), "vP0")
        block.vector(b_vec3)
        assert _stop["on"] or cnt["vTN"] == 8
        vP0_alpha_in = cnt["vP0"]

        def b_tens2(tensor):
            for r in range(8):
                tensor.wait_ge(S["vTN"], r + 1)
                wg(tensor, "sTN", 2 * r)
                for c in range(8):
                    dst = (pstA if c < 4 else pstB)
                    mm = tensor.matmul(dst[:, (c % 4) * 128:(c % 4 + 1) * 128],
                                       tnat[:, (r % 2) * 1024 + c * 128:
                                            (r % 2) * 1024 + (c + 1) * 128],
                                       id16[:], start=True, stop=True)
                    if c == 3 or c == 7:
                        winc(mm, "tTN")
            tensor.wait_ge(S["vP0"], vP0_alpha_in)
            winc(tensor.matmul(ps0[0:1, 2:3], ones_c[:], nsum[:], start=True, stop=True),
                 "tP0")
            winc(tensor.matmul(ps0[0:1, 3:4], ones_c[:], s1sum[:], start=True, stop=True),
                 "tP0")
        block.tensor(b_tens2)
        assert _stop["on"] or cnt["tTN"] == 16
        tP0_alpha = cnt["tP0"]

        def b_scal(scalar):
            for r in range(8):
                for grp in range(2):
                    scalar.wait_ge(S["tTN"], 2 * r + grp + 1)
                    if r >= 2:
                        wg(scalar, f"dTN{r % 2}", 16 * (r // 2))
                    winc(scalar.activation(
                        ttev[:, (r % 2) * 1024 + grp * 512:(r % 2) * 1024 + (grp + 1) * 512],
                        (pstA if grp == 0 else pstB)[:], AF.Copy), "sTN")
        block.scalar(b_scal)
        assert _stop["on"] or cnt["sTN"] == 16

        def b_gp(gpsimd):
            for r in range(8):
                gpsimd.wait_ge(S["sTN"], 2 * r + 2)
                winc(gpsimd.dma_start(
                    out=tern_d.rearrange("(c p) o -> p c o", p=128)[:, :, r * 128:(r + 1) * 128],
                    in_=tnat_view_ttev(r)), f"dTN{r % 2}", 16)
        def tnat_view_ttev(r):
            return ttev[:, (r % 2) * 1024:(r % 2 + 1) * 1024].rearrange(
                "p (c o) -> p c o", o=128)
        block.gpsimd(b_gp)
        assert _stop["on"] or cnt["dTN0"] + cnt["dTN1"] == 128

        def b_vec4(vector):
            vector.wait_ge(S["tP0"], tP0_alpha)
            vector.tensor_copy(scal[0:1, 8:9], ps0[0:1, 3:4])
            vector.tensor_scalar(scal[0:1, 7:8], ps0[0:1, 2:3], 1.0, None, OP.max)
            vector.drain()
            vector.reciprocal(scal[0:1, 9:10], scal[0:1, 7:8])
            vector.drain()
            winc(vector.tensor_tensor(scal[0:1, 6:7], scal[0:1, 8:9], scal[0:1, 9:10],
                                      OP.mult), "vP0")
        block.vector(b_vec4)
        vP0_alpha = cnt["vP0"]

        def b_tens_a(tensor):
            tensor.wait_ge(S["vP0"], vP0_alpha)
            winc(tensor.matmul(ps0[:, 5:6], ones_r[:], scal[0:1, 6:7],
                               start=True, stop=True), "tP0")
        block.tensor(b_tens_a)
        tP0_ab = cnt["tP0"]

        def b_vec5(vector):
            vector.wait_ge(S["tP0"], tP0_ab)
            winc(vector.tensor_copy(scal[:, 4:5], ps0[:, 5:6]), "vP0")
        block.vector(b_vec5)
        vP0_ab = cnt["vP0"]

        ckpt("p0", [("dTN0", 64), ("dTN1", 64), ("vP0", cnt["vP0"]), ("sTN", 16),
                    ("tP0", cnt["tP0"])])
        p0.close()

        for b in range(NB):
            bs = ExitStack()
            ctxT = bs.enter_context(nc.sbuf_tensor(f"ctxT{b}", [128, 16384], BF16))

            qs = ExitStack()
            QT = qs.enter_context(nc.sbuf_tensor(f"QT{b}", [128, 16384], BF16))
            KT = qs.enter_context(nc.sbuf_tensor(f"KT{b}", [128, 16384], BF16))
            Vn = qs.enter_context(nc.sbuf_tensor(f"Vn{b}", [128, 16384], BF16))
            bv_b = qs.enter_context(nc.sbuf_tensor(f"bv_b{b}", [128, 1024], F32))

            pa = ExitStack()
            xT = pa.enter_context(nc.sbuf_tensor(f"xT{b}", [128, 16384], BF16))
            Wb = pa.enter_context(nc.sbuf_tensor(f"Wb{b}", [128, 8192], BF16))
            xbf = pa.enter_context(nc.sbuf_tensor(f"xbf{b}", [128, 2048], BF16))
            pps = [pa.enter_context(nc.psum_tensor(f"pp{b}_{i}", [128, 512], F32))
                   for i in range(8)]

            def p1_gp(gpsimd, b=b):
                wg(gpsimd, "dS1", 256 * b)
                wg(gpsimd, "dS20", 128 * b)
                wg(gpsimd, "dS21", 128 * b)
                wg(gpsimd, "s_ln2", 16 * b)
                if b == 0:
                    gpsimd.wait_ge(S["dTN0"], 64)
                    gpsimd.wait_ge(S["dTN1"], 64)
                for t in range(NT):
                    wg(gpsimd, "t_p1", 32 * b + 2 * t - 2)
                    winc(gpsimd.dma_start(
                        out=xbf[:, (t % 2) * 1024:(t % 2 + 1) * 1024],
                        in_=x_in[b * 2048 + t * 128: b * 2048 + (t + 1) * 128, :]),
                        f"dXB{t % 2}", 16)
            block.gpsimd(p1_gp)
            assert _stop["on"] or cnt["dXB0"] + cnt["dXB1"] == 16 * (16 * b + 16)

            def p1_tens(tensor, b=b):
                wg(tensor, "v_p4e", 16 * b)
                if b == 0:
                    tensor.wait_ge(S["vP0"], vP0_ab)
                    tensor.wait_ge(S["sTN"], 16)
                    tensor.wait_ge(S["dTN0"], 64)
                    tensor.wait_ge(S["dTN1"], 64)
                for t in range(NT):
                    tensor.wait_ge(S[f"dXB{t % 2}"], 16 * (8 * b + t // 2 + 1))
                    wg(tensor, "s_p1", 32 * b + 2 * t - 2)
                    for c in range(8):
                        dst = pps[2 * (t % 2) + c // 4]
                        mm = tensor.matmul(dst[:, (c % 4) * 128:(c % 4 + 1) * 128],
                                           xbf[:, (t % 2) * 1024 + c * 128:
                                               (t % 2) * 1024 + (c + 1) * 128],
                                           id16[:], start=True, stop=True)
                        if c == 3 or c == 7:
                            winc(mm, "t_p1")
            block.tensor(p1_tens)
            assert _stop["on"] or cnt["t_p1"] == 32 * b + 32
            ckpt(f"p1t{b}", [("t_p1", 32 * b + 32)])

            def p1_scal(scalar, b=b):
                for t in range(NT):
                    for grp in range(2):
                        scalar.wait_ge(S["t_p1"], 32 * b + 2 * t + grp + 1)
                        dstv = xT[:].rearrange("p (c n) -> p c n", n=2048)[
                            :, 4 * grp:4 * grp + 4, t * 128:(t + 1) * 128]
                        winc(scalar.activation(dstv, pps[2 * (t % 2) + grp][:], AF.Copy),
                             "s_p1")
            block.scalar(p1_scal)
            assert _stop["on"] or cnt["s_p1"] == 32 * b + 32
            ckpt(f"p1b{b}", [("s_p1", 32 * b + 32)])

            def p2_gp(gpsimd, b=b):
                for i, w_d in enumerate([wq_d, wk_d, wv_d]):
                    wg(gpsimd, "t_qkv", 32 * b + 8 * i)
                    winc(gpsimd.dma_start(
                        out=Wb[:], in_=w_d.rearrange("(c p) n -> p c n", p=128)),
                        "dW", 16)
            block.gpsimd(p2_gp)
            assert _stop["on"] or cnt["dW"] == 16 * (3 * b + 3)

            def p2_sync(sync, b=b):
                wg(sync, "t_qkv", 32 * b + 16)
                winc(sync.dma_start(
                    out=bv_b[:],
                    in_=bv_d.rearrange("(a n) -> a n", a=1).partition_broadcast(128).squeeze()),
                    "dRow", 16)
            block.sync(p2_sync)
            dRow_bv = cnt["dRow"]

            def p2_tens(tensor, b=b):
                tensor.wait_ge(S["s_p1"], 32 * b + 32)
                for stage in range(2):
                    tensor.wait_ge(S["dW"], 16 * (3 * b + stage + 1))
                    if stage == 1:
                        tensor.wait_ge(S["s_qt"], 32 * b + 32)
                    ssem = "s_qt" if stage == 0 else "s_kt"
                    for g in range(8):
                        wg(tensor, ssem, 32 * b + 4 * (g - 2) + 4)
                        for c in range(8):
                            lhs = Wb[:, c * 1024 + g * 128: c * 1024 + (g + 1) * 128]
                            for n4 in range(4):
                                mm = tensor.matmul(
                                    pps[4 * (g % 2) + n4][:],
                                    lhs,
                                    xT[:, c * 2048 + n4 * 512: c * 2048 + (n4 + 1) * 512],
                                    start=(c == 0), stop=(c == 7))
                        winc(mm, "t_qkv")
                tensor.wait_ge(S["dW"], 16 * (3 * b + 3))
                tensor.wait_ge(S["s_kt"], 32 * b + 32)
                for t in range(NT):
                    wg(tensor, "v_vev", 32 * b + 2 * t - 2)
                    for c in range(8):
                        lhs = xT[:, c * 2048 + t * 128: c * 2048 + (t + 1) * 128]
                        for h in range(2):
                            mm = tensor.matmul(
                                pps[2 * (t % 2) + h][:], lhs,
                                Wb[:, c * 1024 + h * 512: c * 1024 + (h + 1) * 512],
                                start=(c == 0), stop=(c == 7))
                    winc(mm, "t_qkv")
            block.tensor(p2_tens)
            assert _stop["on"] or cnt["t_qkv"] == 32 * b + 32

            def p2_scal(scalar, b=b):
                for stage in range(2):
                    dst = QT if stage == 0 else KT
                    bias = bqt if stage == 0 else bkt
                    ssem = "s_qt" if stage == 0 else "s_kt"
                    for g in range(8):
                        scalar.wait_ge(S["t_qkv"], 32 * b + 8 * stage + g + 1)
                        for n4 in range(4):
                            winc(scalar.activation(
                                dst[:, g * 2048 + n4 * 512: g * 2048 + (n4 + 1) * 512],
                                pps[4 * (g % 2) + n4][:], AF.Identity,
                                bias=bias[:, g:g + 1], scale=1.0), ssem)
            block.scalar(p2_scal)
            assert _stop["on"] or cnt["s_qt"] == 32 * b + 32 and cnt["s_kt"] == 32 * b + 32

            def p2_vec(vector, b=b):
                vector.wait_ge(S["dRow"], dRow_bv)
                for t in range(NT):
                    vector.wait_ge(S["t_qkv"], 32 * b + 16 + t + 1)
                    for h in range(2):
                        winc(vector.scalar_tensor_tensor(
                            Vn[:, t * 1024 + h * 512: t * 1024 + (h + 1) * 512],
                            pps[2 * (t % 2) + h][:], 1.0,
                            bv_b[:, h * 512:(h + 1) * 512], OP.mult, OP.add), "v_vev")
            block.vector(p2_vec)
            assert _stop["on"] or cnt["v_vev"] == 32 * b + 32
            ckpt(f"p2b{b}", [("s_qt", 32 * b + 32), ("s_kt", 32 * b + 32),
                             ("v_vev", 32 * b + 32)])

            pa.close()

            pb = ExitStack()
            E0 = pb.enter_context(nc.sbuf_tensor(f"E0_{b}", [128, 4096], BF16))
            E1 = pb.enter_context(nc.sbuf_tensor(f"E1_{b}", [128, 4096], BF16))
            R0 = pb.enter_context(nc.sbuf_tensor(f"R0_{b}", [128, 4096], BF16))
            R1 = pb.enter_context(nc.sbuf_tensor(f"R1_{b}", [128, 4096], BF16))
            z1 = pb.enter_context(nc.sbuf_tensor(f"z1_{b}", [128, 2048], BF16))
            z2 = pb.enter_context(nc.sbuf_tensor(f"z2_{b}", [128, 1024], BF16))
            Zt = [pb.enter_context(nc.sbuf_tensor(f"Z{i}_{b}", [128, 512], BF16))
                  for i in range(2)]
            lnz = pb.enter_context(nc.sbuf_tensor(f"lnz_{b}", [128, 512], F32))
            ivz = [pb.enter_context(nc.sbuf_tensor(f"ivz{i}_{b}", [128, 512], BF16))
                   for i in range(2)]
            scp = [pb.enter_context(nc.psum_tensor(f"sc{b}_{i}", [128, 512], F32))
                   for i in range(3)]
            ctxp = [pb.enter_context(nc.psum_tensor(f"cx{b}_{i}", [128, 512], F32))
                    for i in range(4)]
            Es = [E0, E1]
            Rs = [R0, R1]

            def p3_tens(tensor, b=b):
                tensor.wait_ge(S["s_qt"], 32 * b + 32)
                tensor.wait_ge(S["s_kt"], 32 * b + 32)
                tensor.wait_ge(S["v_vev"], 32 * b + 32)
                for u in range(NU):
                    U = NU * b + u
                    nch, mtp = u // 8, u % 8
                    for mti in range(2):
                        mt = 2 * mtp + mti
                        for p4 in range(4):
                            BK = 8 * U + mti * 4 + p4
                            wg(tensor, "s_exp", BK - 2)
                            for gpar in range(2):
                                g = 2 * p4 + gpar
                                mm = tensor.matmul(
                                    scp[BK % 3][:, gpar * 256:(gpar + 1) * 256],
                                    KT[:, g * 2048 + mt * 128: g * 2048 + (mt + 1) * 128],
                                    QT[:, g * 2048 + nch * 256: g * 2048 + (nch + 1) * 256],
                                    start=True, stop=True)
                            winc(mm, "t_sc")
                    if u >= 1:
                        emit_pv(tensor, b, u - 1)
                emit_pv(tensor, b, NU - 1)

            def emit_pv(tensor, b, v):
                V_ = NU * b + v
                nchv, mtpv = v // 8, v % 8
                tensor.wait_ge(S["v_r"], V_ + 1)
                if v % 8 == 0:
                    wg(tensor, "s_ctx", 32 * b + 4 * nchv)
                Rv = Rs[V_ % 2][:].rearrange("p (pp mti g n) -> p pp mti g n",
                                             mti=2, g=2, n=256)
                mm = None
                for p4 in range(4):
                    for gpar in range(2):
                        g = 2 * p4 + gpar
                        for mti in range(2):
                            mt = 2 * mtpv + mti
                            mm = tensor.matmul(
                                ctxp[g // 2][:, (g % 2) * 256:(g % 2) * 256 + 256],
                                Vn[:, mt * 1024 + g * 128: mt * 1024 + (g + 1) * 128],
                                Rv[:, p4, mti, gpar, :],
                                start=(mtpv == 0 and mti == 0 and gpar == 0),
                                stop=(mtpv == 7 and mti == 1 and gpar == 1))
                winc(mm, "t_pv")

            block.tensor(p3_tens)
            assert _stop["on"] or cnt["t_sc"] == 8 * NU * (b + 1), cnt["t_sc"]
            assert _stop["on"] or cnt["t_pv"] == NU * (b + 1)

            def p3_scal(scalar, b=b):
                for u in range(NU):
                    U = NU * b + u
                    wg(scalar, "v_r", U - 1)
                    for mti in range(2):
                        for p4 in range(4):
                            BK = 8 * U + mti * 4 + p4
                            scalar.wait_ge(S["t_sc"], BK + 1)
                            winc(scalar.activation(
                                Es[U % 2][:, p4 * 1024 + mti * 512:
                                          p4 * 1024 + (mti + 1) * 512],
                                scp[BK % 3][:], AF.Exp, scale=float(1.0 / 32.0)),
                                "s_exp")
                    if u >= 1:
                        emit_invz(scalar, b, u - 1)
                emit_invz(scalar, b, NU - 1)

            def emit_invz(scalar, b, w):
                W_ = NU * b + w
                scalar.wait_ge(S["v_z"], W_ + 1)
                scalar.activation(lnz[:], Zt[W_ % 2][:], AF.Ln)
                scalar.drain()
                winc(scalar.activation(ivz[W_ % 2][:], lnz[:], AF.Exp, scale=-1.0),
                     "s_inv")
                scalar.drain()
                if w % 8 == 7:
                    nchw = w // 8
                    scalar.wait_ge(S["t_pv"], NU * b + 8 * nchw + 8)
                    for j in range(4):
                        dstv = ctxT[:].rearrange("p (g n) -> p g n", n=2048)[
                            :, 2 * j:2 * j + 2, nchw * 256:(nchw + 1) * 256]
                        winc(scalar.activation(dstv, ctxp[j][:], AF.Copy), "s_ctx")

            block.scalar(p3_scal)
            assert _stop["on"] or cnt["s_exp"] == 8 * NU * (b + 1)
            assert _stop["on"] or cnt["s_inv"] == NU * (b + 1)
            assert _stop["on"] or cnt["s_ctx"] == 32 * (b + 1)

            def p3_vec(vector, b=b):
                for u in range(NU):
                    U = NU * b + u
                    vector.wait_ge(S["s_exp"], 8 * U + 8)
                    wg(vector, "s_inv", U - 1)
                    Ev = Es[U % 2][:].rearrange("p (bb k n) -> p bb k n", k=2, n=256)
                    z1v = z1[:].rearrange("p (bb n) -> p bb n", n=256)
                    vector.tensor_tensor(z1v, Ev[:, :, 0, :], Ev[:, :, 1, :], OP.add)
                    vector.drain()
                    vector.tensor_tensor(z2[:], z1[:, 0:1024], z1[:, 1024:2048], OP.add)
                    vector.drain()
                    winc(vector.tensor_tensor(Zt[U % 2][:], z2[:, 0:512],
                                              z2[:, 512:1024], OP.add), "v_z")
                    if u >= 1:
                        emit_r(vector, b, u - 1)
                emit_r(vector, b, NU - 1)

            def emit_r(vector, b, w):
                W_ = NU * b + w
                vector.wait_ge(S["s_inv"], W_ + 1)
                wg(vector, "t_pv", W_ - 1)
                Ev = Es[W_ % 2][:].rearrange("p (pp mti g n) -> p pp mti g n",
                                             mti=2, g=2, n=256)
                Rv = Rs[W_ % 2][:].rearrange("p (pp mti g n) -> p pp mti g n",
                                             mti=2, g=2, n=256)
                iv = ivz[W_ % 2][:].rearrange("p (mti n) -> p mti n", n=256)
                for p4 in range(4):
                    for gpar in range(2):
                        mm = vector.tensor_tensor(Rv[:, p4, :, gpar, :],
                                                  Ev[:, p4, :, gpar, :], iv, OP.mult)
                winc(mm, "v_r")

            block.vector(p3_vec)
            assert _stop["on"] or cnt["v_z"] == NU * (b + 1)
            assert _stop["on"] or cnt["v_r"] == NU * (b + 1)
            ckpt(f"p3b{b}", [("s_ctx", 32 * (b + 1)), ("t_pv", NU * (b + 1)),
                             ("v_r", NU * (b + 1))])

            pb.close()
            qs.close()

            pc = ExitStack()
            x1f = pc.enter_context(nc.sbuf_tensor(f"x1f{b}", [128, 16384], F32))
            xstr = pc.enter_context(nc.sbuf_tensor(f"xstr{b}", [128, 2048], F32))
            xnt = pc.enter_context(nc.sbuf_tensor(f"xnt{b}", [128, 2048], F32))
            xng = pc.enter_context(nc.sbuf_tensor(f"xng{b}", [128, 2048], F32))
            gam_b = pc.enter_context(nc.sbuf_tensor(f"gam_b{b}", [128, 1024], F32))
            bet_b = pc.enter_context(nc.sbuf_tensor(f"bet_b{b}", [128, 1024], F32))
            bn1 = pc.enter_context(nc.sbuf_tensor(f"bn1_{b}", [128, 192], F32))
            bn2 = pc.enter_context(nc.sbuf_tensor(f"bn2_{b}", [128, 192], F32))
            mv1 = pc.enter_context(nc.sbuf_tensor(f"mv1_{b}", [128, 32], F32))
            mv2 = pc.enter_context(nc.sbuf_tensor(f"mv2_{b}", [128, 32], F32))
            r1s = pc.enter_context(nc.sbuf_tensor(f"r1s{b}", [128, 16], F32))
            r2s = pc.enter_context(nc.sbuf_tensor(f"r2s{b}", [128, 16], F32))
            b1s = pc.enter_context(nc.sbuf_tensor(f"b1s{b}", [128, 16], F32))
            b2s = pc.enter_context(nc.sbuf_tensor(f"b2s{b}", [128, 16], F32))
            tt1 = pc.enter_context(nc.sbuf_tensor(f"tt1_{b}", [128, 16], F32))
            p4ps = [pc.enter_context(nc.psum_tensor(f"p4_{b}_{i}", [128, 512], F32))
                    for i in range(4)]

            def p4_sync(sync, b=b):
                sync.wait_ge(S["s_ctx"], 32 * b + 32)
                for t in range(NT):
                    wg(sync, "v_p4a", 16 * b + t - 1)
                    winc(sync.dma_start(
                        out=xstr[:, (t % 2) * 1024:(t % 2 + 1) * 1024],
                        in_=x_in[b * 2048 + t * 128: b * 2048 + (t + 1) * 128, :]),
                        f"dXF{t % 2}", 16)
                for k, src in enumerate([gam_d, bet_d]):
                    winc(sync.dma_start(
                        out=(gam_b if k == 0 else bet_b)[:],
                        in_=src.rearrange("(a n) -> a n", a=1).partition_broadcast(128).squeeze()),
                        "dRow", 16)
            block.sync(p4_sync)
            dRow_gb = cnt["dRow"]

            def p4_tens(tensor, b=b):
                tensor.wait_ge(S["s_ctx"], 32 * b + 32)
                tensor.wait_ge(S["s_exp"], 8 * NU * (b + 1))
                for t in range(NT):
                    wg(tensor, "v_p4a", 16 * b + t - 1)
                    for g in range(8):
                        dst = p4ps[2 * (t % 2) + g // 4]
                        mm = tensor.matmul(
                            dst[:, (g % 4) * 128:(g % 4 + 1) * 128],
                            ctxT[:, g * 2048 + t * 128: g * 2048 + (t + 1) * 128],
                            id16[:], start=True, stop=True)
                    winc(mm, "t_p4")
            block.tensor(p4_tens)
            assert _stop["on"] or cnt["t_p4"] == 16 * (b + 1)

            def p4_vecA(vector, b=b):
                for t in range(NT):
                    vector.wait_ge(S["t_p4"], 16 * b + t + 1)
                    vector.wait_ge(S[f"dXF{t % 2}"], 16 * (8 * b + t // 2 + 1))
                    for h in range(2):
                        vector.tensor_tensor(
                            x1f[:, t * 1024 + h * 512: t * 1024 + (h + 1) * 512],
                            xstr[:, (t % 2) * 1024 + h * 512:(t % 2) * 1024 + (h + 1) * 512],
                            p4ps[2 * (t % 2) + h][:], OP.add)
                    vector.drain()
                    for h in range(2):
                        vector.bn_stats(bn1[:, 12 * t + 6 * h: 12 * t + 6 * h + 6],
                                        x1f[:, t * 1024 + h * 512: t * 1024 + (h + 1) * 512])
                    winc(vector.nop(), "v_p4a")
                vector.drain()
                for t in range(NT):
                    vector.bn_aggr(mv1[:, 2 * t:2 * t + 2], bn1[:, 12 * t:12 * t + 12])
                vector.drain()
                vview = mv1[:].rearrange("p (t k) -> p t k", k=2)
                winc(vector.tensor_scalar(tt1[:], vview[:, :, 1], EPS, None, OP.add),
                     "v_p4b")
            block.vector(p4_vecA)
            v_p4b_v = cnt["v_p4b"]

            def p4_scalB(scalar, b=b):
                scalar.wait_ge(S["v_p4b"], v_p4b_v)
                scalar.activation(r1s[:], tt1[:], AF.Ln)
                scalar.drain()
                winc(scalar.activation(r1s[:], r1s[:], AF.Exp, scale=-0.5), "s_p4b")
            block.scalar(p4_scalB)
            s_p4b_v = cnt["s_p4b"]

            def p4_vecB(vector, b=b):
                vector.wait_ge(S["s_p4b"], s_p4b_v)
                mview = mv1[:].rearrange("p (t k) -> p t k", k=2)
                vector.tensor_tensor(b1s[:], mview[:, :, 0], r1s[:], OP.mult)
                vector.drain()
                winc(vector.tensor_scalar(b1s[:], b1s[:], -1.0, None, OP.mult), "v_p4b2")
            block.vector(p4_vecB)
            v_p4b2_v = cnt["v_p4b2"]

            def p4_scalC(scalar, b=b):
                scalar.wait_ge(S["v_p4b2"], v_p4b2_v)
                for t in range(NT):
                    sl = x1f[:, t * 1024:(t + 1) * 1024]
                    winc(scalar.activation(sl, sl, AF.Identity, scale=r1s[:, t:t + 1],
                                           bias=b1s[:, t:t + 1]), "s_ln1")
            block.scalar(p4_scalC)
            assert _stop["on"] or cnt["s_ln1"] == 16 * (b + 1)

            def p4_gpC(gpsimd, b=b):
                for t in range(NT):
                    gpsimd.wait_ge(S["s_ln1"], 16 * b + t + 1)
                    winc(gpsimd.dma_start(
                        out=x1sp_d[b * 2048 + t * 128: b * 2048 + (t + 1) * 128, :],
                        in_=x1f[:, t * 1024:(t + 1) * 1024]), "dS1", 16)
            block.gpsimd(p4_gpC)
            assert _stop["on"] or cnt["dS1"] == 256 * (b + 1)

            def p4_vecC(vector, b=b):
                for t in range(NT):
                    vector.wait_ge(S["s_ln1"], 16 * b + t + 1)
                    for h in range(2):
                        vector.bn_stats(bn2[:, 12 * t + 6 * h: 12 * t + 6 * h + 6],
                                        x1f[:, t * 1024 + h * 512: t * 1024 + (h + 1) * 512])
                    winc(vector.nop(), "v_p4c")
                vector.drain()
                for t in range(NT):
                    vector.bn_aggr(mv2[:, 2 * t:2 * t + 2], bn2[:, 12 * t:12 * t + 12])
                vector.drain()
                vview = mv2[:].rearrange("p (t k) -> p t k", k=2)
                winc(vector.tensor_scalar(tt1[:], vview[:, :, 1], EPS, None, OP.add),
                     "v_p4d")
            block.vector(p4_vecC)
            v_p4d_v = cnt["v_p4d"]

            def p4_scalD(scalar, b=b):
                scalar.wait_ge(S["v_p4d"], v_p4d_v)
                scalar.activation(r2s[:], tt1[:], AF.Ln)
                scalar.drain()
                winc(scalar.activation(r2s[:], r2s[:], AF.Exp, scale=-0.5), "s_p4d")
            block.scalar(p4_scalD)
            s_p4d_v = cnt["s_p4d"]

            def p4_vecD(vector, b=b):
                vector.wait_ge(S["s_p4d"], s_p4d_v)
                mview = mv2[:].rearrange("p (t k) -> p t k", k=2)
                vector.tensor_tensor(b2s[:], mview[:, :, 0], r2s[:], OP.mult)
                vector.drain()
                winc(vector.tensor_scalar(b2s[:], b2s[:], -1.0, None, OP.mult), "v_p4d2")
            block.vector(p4_vecD)
            v_p4d2_v = cnt["v_p4d2"]

            def p4_scalE(scalar, b=b):
                scalar.wait_ge(S["v_p4d2"], v_p4d2_v)
                scalar.drain()
                for t in range(NT):
                    wg(scalar, "v_p4e", 16 * b + t - 1)
                    winc(scalar.activation(
                        xnt[:, (t % 2) * 1024:(t % 2 + 1) * 1024],
                        x1f[:, t * 1024:(t + 1) * 1024], AF.Identity,
                        scale=r2s[:, t:t + 1], bias=b2s[:, t:t + 1]), "s_ln2")
                    scalar.drain()
            block.scalar(p4_scalE)
            assert _stop["on"] or cnt["s_ln2"] == 16 * (b + 1)

            def p4_vecE(vector, b=b):
                vector.wait_ge(S["dRow"], dRow_gb)
                for t in range(NT):
                    vector.wait_ge(S["s_ln2"], 16 * b + t + 1)
                    if 8 * b + t // 2 >= 1:
                        wg(vector, f"dS2{t % 2}", 16 * (8 * b + t // 2))
                    sl_in = xnt[:, (t % 2) * 1024:(t % 2 + 1) * 1024]
                    sl = xng[:, (t % 2) * 1024:(t % 2 + 1) * 1024]
                    vector.tensor_tensor(sl, sl_in, gam_b[:], OP.mult)
                    vector.drain()
                    vector.tensor_tensor(sl, sl, bet_b[:], OP.add)
                    vector.drain()
                    vector.tensor_reduce(mx1[:], sl, AX.X, OP.max,
                                         apply_absolute_value=True)
                    vector.drain()
                    winc(vector.tensor_tensor(scal[:, 0:1], scal[:, 0:1], mx1[:],
                                              OP.max), "v_p4e")
                    vector.drain()
            block.vector(p4_vecE)
            assert _stop["on"] or cnt["v_p4e"] == 16 * (b + 1)

            def p4_gpE(gpsimd, b=b):
                for t in range(NT):
                    gpsimd.wait_ge(S["v_p4e"], 16 * b + t + 1)
                    winc(gpsimd.dma_start(
                        out=xnsp_d[b * 2048 + t * 128: b * 2048 + (t + 1) * 128, :],
                        in_=xng[:, (t % 2) * 1024:(t % 2 + 1) * 1024]), f"dS2{t % 2}", 16)
            block.gpsimd(p4_gpE)
            assert _stop["on"] or cnt["dS20"] + cnt["dS21"] == 256 * (b + 1)
            ckpt(f"p4b{b}", [("dS1", 256 * (b + 1)), ("dS20", 128 * (b + 1)),
                             ("dS21", 128 * (b + 1)), ("v_p4e", 16 * (b + 1))])

            pc.close()
            bs.close()

        sy = ExitStack()
        msps = sy.enter_context(nc.psum_tensor("msps", [128, 512], F32))

        def sy_gp(gpsimd):
            for r, delta in enumerate([1, 2, 4]):
                slot = 4 if delta == 4 else 0
                rdests = [None] * 8
                rdests[slot] = (0, delta)
                if r == 0:
                    gpsimd.wait_ge(S["v_p4e"], 32)
                else:
                    gpsimd.wait_ge(S["v_pr"], r)
                    gpsimd.wait_ge(S["dRL"], 16 * r)
                winc(gpsimd.remote_dma_broadcast(
                    out_ap=inbox[:, r:r + 1], in_ap=scal[:, 0:1],
                    remote_sem=S[f"rs{r}"], local_sem=S["dRL"],
                    rdests=rdests), "r_prep")
                cnt["dRL"] += 16
                gpsimd.wait_ge(S["r_prep"], r + 1)
                gpsimd.trigger_dma(1)
        block.gpsimd(sy_gp)

        def sy_vec(vector):
            for r in range(3):
                vector.wait_ge(S[f"rs{r}"], 2)
                vector.wait_ge(S["dRL"], 16 * (r + 1))
                winc(vector.tensor_tensor(scal[:, 0:1], scal[:, 0:1],
                                          inbox[:, r:r + 1], OP.max), "v_pr")
                vector.drain()
        block.vector(sy_vec)
        assert _stop["on"] or cnt["v_pr"] == 3

        def sy_tens(tensor):
            tensor.wait_ge(S["v_pr"], 3)
            tensor.wait_ge(S["v_p4a"], 32)
            winc(tensor.matmul(msps[0:1, 0:128], scal[:, 0:1], idf[:],
                               start=True, stop=True), "t_sy")
        block.tensor(sy_tens)

        def sy_vec2(vector):
            vector.wait_ge(S["t_sy"], 1)
            vector.tensor_reduce(scal[0:1, 5:6], msps[0:1, 0:128], AX.X, OP.max)
            vector.drain()
            winc(vector.tensor_scalar(scal[0:1, 5:6], scal[0:1, 5:6], 1e-8, None,
                                      OP.max), "v_sy")
        block.vector(sy_vec2)

        def sy_tens2(tensor):
            tensor.wait_ge(S["v_sy"], 1)
            winc(tensor.matmul(msps[:, 128:129], ones_r[:], scal[0:1, 5:6],
                               start=True, stop=True), "t_sy")
        block.tensor(sy_tens2)

        def sy_vec3(vector):
            vector.wait_ge(S["t_sy"], 2)
            vector.tensor_copy(scal[:, 1:2], msps[:, 128:129])
            vector.drain()
            vector.reciprocal(scal[:, 2:3], scal[:, 1:2])
            vector.tensor_tensor(scal[:, 3:4], scal[:, 1:2], scal[:, 4:5], OP.mult)
            vector.drain()
            vector.tensor_scalar(scal[:, 2:3], scal[:, 2:3], 127.0, None, OP.mult)
            winc(vector.tensor_scalar(scal[:, 3:4], scal[:, 3:4],
                                      float(1.0 / 127.0), None, OP.mult), "v_gm")
            vector.drain()
        block.vector(sy_vec3)
        ckpt("sync", [("v_gm", 1)])
        sy.close()

        p5 = ExitStack()
        xqT = p5.enter_context(nc.sbuf_tensor("xqT", [128, 16384], BF16))
        ternT = p5.enter_context(nc.sbuf_tensor("ternT", [128, 8192], BF16))
        t2f = p5.enter_context(nc.sbuf_tensor("t2f", [128, 16384], F32))
        tqb = p5.enter_context(nc.sbuf_tensor("tqb", [128, 2048], F32))
        xqn = p5.enter_context(nc.sbuf_tensor("xqn", [128, 2048], BF16))
        xnstr = p5.enter_context(nc.sbuf_tensor("xnstr", [128, 2048], F16))
        x1str = p5.enter_context(nc.sbuf_tensor("x1str", [128, 2048], F16))
        outsb = p5.enter_context(nc.sbuf_tensor("outsb", [128, 2048], F32))
        bn3 = p5.enter_context(nc.sbuf_tensor("bn3", [128, 192], F32))
        mv3 = p5.enter_context(nc.sbuf_tensor("mv3", [128, 32], F32))
        r3s = p5.enter_context(nc.sbuf_tensor("r3s", [128, 16], F32))
        b3s = p5.enter_context(nc.sbuf_tensor("b3s", [128, 16], F32))
        tt3 = p5.enter_context(nc.sbuf_tensor("tt3", [128, 16], F32))
        p5tp = [p5.enter_context(nc.psum_tensor(f"p5t{i}", [128, 512], F32))
                for i in range(4)]
        f5ps = [p5.enter_context(nc.psum_tensor(f"f5{i}", [128, 512], F32))
                for i in range(4)]

        def p5_sync0(sync):
            sync.wait_ge(S["v_gm"], 1)
            sync.wait_ge(S["dTN0"], 64)
            sync.wait_ge(S["dTN1"], 64)
            sync.wait_ge(S["dS1"], 512)
            sync.wait_ge(S["dS20"], 256)
            sync.wait_ge(S["dS21"], 256)
            winc(sync.dma_start(out=ternT[:],
                                in_=tern_d.rearrange("(c p) o -> p c o", p=128)),
                 "dT5", 16)
        block.sync(p5_sync0)

        for bh in range(NB):
            def p5_sync(sync, bh=bh):
                for t in range(NT):
                    rows = slice(bh * 2048 + t * 128, bh * 2048 + (t + 1) * 128)
                    wg(sync, "v_q", 16 * bh + t - 1)
                    winc(sync.dma_start(out=xnstr[:, (t % 2) * 1024:(t % 2 + 1) * 1024],
                                        in_=xnsp_d[rows, :]), f"dXNa{t % 2}", 16)
            block.sync(p5_sync)

            def p5_gp(gpsimd, bh=bh):
                if bh == 0:
                    gpsimd.wait_ge(S["v_gm"], 1)
                for t in range(NT):
                    rows = slice(bh * 2048 + t * 128, bh * 2048 + (t + 1) * 128)
                    wg(gpsimd, "v_ffn", 16 * bh + t - 1)
                    winc(gpsimd.dma_start(out=x1str[:, (t % 2) * 1024:(t % 2 + 1) * 1024],
                                          in_=x1sp_d[rows, :]), f"dXNb{t % 2}", 16)
            block.gpsimd(p5_gp)

            def p5_stt(vector, bh, w):
                vector.wait_ge(S["t_ffn"], 16 * bh + w + 1)
                vector.wait_ge(S[f"dXNb{w % 2}"], 16 * (8 * bh + w // 2 + 1))
                if bh == 1 and w == 0:
                    vector.wait_ge(S["s_ln3"], 16)
                for h in range(2):
                    stt = vector.scalar_tensor_tensor(
                        t2f[:, w * 1024 + h * 512: w * 1024 + (h + 1) * 512],
                        f5ps[2 * (w % 2) + h][:], scal[:, 3:4],
                        x1str[:, (w % 2) * 1024 + h * 512:(w % 2) * 1024 + (h + 1) * 512],
                        OP.mult, OP.add)
                    if h == 1:
                        winc(stt, "v_ffn")
                vector.drain()
                for h in range(2):
                    vector.bn_stats(bn3[:, 12 * w + 6 * h: 12 * w + 6 * h + 6],
                                    t2f[:, w * 1024 + h * 512: w * 1024 + (h + 1) * 512])

            def p5_vecQ(vector, bh=bh):
                if bh == 0:
                    vector.wait_ge(S["v_gm"], 1)
                for t in range(NT):
                    vector.wait_ge(S[f"dXNa{t % 2}"], 16 * (8 * bh + t // 2 + 1))
                    wg(vector, "t_tr5", 16 * bh + t - 1)
                    sl = xnstr[:, (t % 2) * 1024:(t % 2 + 1) * 1024]
                    tq = tqb[:, (t % 2) * 1024:(t % 2 + 1) * 1024]
                    vector.tensor_scalar(tq, sl, scal[:, 2:3], MAGIC, OP.mult, OP.add)
                    vector.drain()
                    winc(vector.tensor_scalar(
                        xqn[:, (t % 2) * 1024:(t % 2 + 1) * 1024], tq, MAGIC, None,
                        OP.subtract), "v_q")
                    if t >= 2:
                        p5_stt(vector, bh, t - 2)
                p5_stt(vector, bh, NT - 2)
                p5_stt(vector, bh, NT - 1)
                vector.drain()
                for t in range(NT):
                    vector.bn_aggr(mv3[:, 2 * t:2 * t + 2], bn3[:, 12 * t:12 * t + 12])
                vector.drain()
                vview = mv3[:].rearrange("p (t k) -> p t k", k=2)
                winc(vector.tensor_scalar(tt3[:], vview[:, :, 1], EPS, None, OP.add),
                     "v_p5b")
            block.vector(p5_vecQ)
            assert _stop["on"] or cnt["v_q"] == 16 * (bh + 1)
            assert _stop["on"] or cnt["v_ffn"] == 16 * (bh + 1)

            def p5_tensT(tensor, bh=bh):
                if bh == 0:
                    tensor.wait_ge(S["v_gm"], 1)
                for t in range(NT):
                    tensor.wait_ge(S["v_q"], 16 * bh + t + 1)
                    wg(tensor, "s_xq", 32 * bh + 2 * t - 2)
                    for c in range(8):
                        dst = p5tp[2 * (t % 2) + c // 4]
                        mm = tensor.matmul(
                            dst[:, (c % 4) * 128:(c % 4 + 1) * 128],
                            xqn[:, (t % 2) * 1024 + c * 128:(t % 2) * 1024 + (c + 1) * 128],
                            id16[:], start=True, stop=True)
                    winc(mm, "t_tr5")
                    tensor.wait_ge(S["s_xq"], 32 * bh + 2 * t + 2)
                    if bh == 0 and t == 0:
                        tensor.wait_ge(S["dT5"], 16)
                    wg(tensor, "v_ffn", 16 * bh + t - 1)
                    for c in range(8):
                        lhs = xqT[:, c * 2048 + t * 128: c * 2048 + (t + 1) * 128]
                        for h in range(2):
                            mm = tensor.matmul(
                                f5ps[2 * (t % 2) + h][:], lhs,
                                ternT[:, c * 1024 + h * 512: c * 1024 + (h + 1) * 512],
                                start=(c == 0), stop=(c == 7))
                    winc(mm, "t_ffn")
            block.tensor(p5_tensT)
            assert _stop["on"] or cnt["t_tr5"] == 16 * (bh + 1)
            assert _stop["on"] or cnt["t_ffn"] == 16 * (bh + 1)

            def p5_scalT(scalar, bh=bh):
                for t in range(NT):
                    for grp in range(2):
                        scalar.wait_ge(S["t_tr5"], 16 * bh + t + 1)
                        dstv = xqT[:].rearrange("p (c n) -> p c n", n=2048)[
                            :, 4 * grp:4 * grp + 4, t * 128:(t + 1) * 128]
                        winc(scalar.activation(dstv, p5tp[2 * (t % 2) + grp][:],
                                               AF.Copy), "s_xq")
            block.scalar(p5_scalT)
            assert _stop["on"] or cnt["s_xq"] == 32 * (bh + 1)

            v_p5b_v = cnt["v_p5b"]

            def p5_scalB(scalar, bh=bh, v_p5b_v=v_p5b_v):
                scalar.wait_ge(S["v_p5b"], v_p5b_v)
                scalar.activation(r3s[:], tt3[:], AF.Ln)
                scalar.drain()
                winc(scalar.activation(r3s[:], r3s[:], AF.Exp, scale=-0.5), "s_p5b")
            block.scalar(p5_scalB)
            s_p5b_v = cnt["s_p5b"]

            def p5_vecB(vector, bh=bh, s_p5b_v=s_p5b_v):
                vector.wait_ge(S["s_p5b"], s_p5b_v)
                mview = mv3[:].rearrange("p (t k) -> p t k", k=2)
                vector.tensor_tensor(b3s[:], mview[:, :, 0], r3s[:], OP.mult)
                vector.drain()
                winc(vector.tensor_scalar(b3s[:], b3s[:], -1.0, None, OP.mult), "v_p5b2")
            block.vector(p5_vecB)
            v_p5b2_v = cnt["v_p5b2"]

            def p5_scalO(scalar, bh=bh, v_p5b2_v=v_p5b2_v):
                scalar.wait_ge(S["v_p5b2"], v_p5b2_v)
                for t in range(NT):
                    if 8 * bh + t // 2 >= 1:
                        wg(scalar, f"dOU{t % 2}", 16 * (8 * bh + t // 2))
                    winc(scalar.activation(
                        outsb[:, (t % 2) * 1024:(t % 2 + 1) * 1024],
                        t2f[:, t * 1024:(t + 1) * 1024], AF.Identity,
                        scale=r3s[:, t:t + 1], bias=b3s[:, t:t + 1]), "s_ln3")
            block.scalar(p5_scalO)
            assert _stop["on"] or cnt["s_ln3"] == 16 * (bh + 1)

            def p5_syncO(sync, bh=bh):
                for t in range(NT):
                    sync.wait_ge(S["s_ln3"], 16 * bh + t + 1)
                    winc(sync.dma_start(
                        out=out_d[bh * 2048 + t * 128: bh * 2048 + (t + 1) * 128, :],
                        in_=outsb[:, (t % 2) * 1024:(t % 2 + 1) * 1024]), f"dOU{t % 2}", 16)
            block.sync(p5_syncO)
            ckpt(f"p5b{bh}", [("dOU0", cnt["dOU0"]), ("dOU1", cnt["dOU1"])])
        assert _stop["on"] or cnt["dOU0"] + cnt["dOU1"] == 16 * 32

        def p5_fin(sync):
            sync.wait_ge(S["dOU0"], 16 * 16)
            sync.wait_ge(S["dOU1"], 16 * 16)
        block.sync(p5_fin)

        if _stop["waits"] is not None:
            def _final(sync):
                for sem, v in _stop["waits"]:
                    sync.wait_ge(S[sem], v)
            _block.sync(_final)

        p5.close()

    inst_type_to_lib_mask = {}
    for lib in all_libraries:
        for ttype in lib.instructions:
            inst_type_to_lib_mask[ttype] = inst_type_to_lib_mask.get(ttype, 0) | (1 << lib.index)
    _bass_rust.insert_library_loads(nc, inst_type_to_lib_mask, len(all_libraries),
                                    standard.index)
    mybir.codegen_inst_isa_subclasses(nc)
    return nc


_NC_CACHE = None


def kernel(L0_attn_hidden_states, L0_attn_Wq, L0_attn_bq, L0_attn_Wk, L0_attn_bk,
           L0_attn_Wv, L0_attn_bv, L0_ffn_weight, L0_ffn_gamma, L0_ffn_beta):
    global _NC_CACHE
    x = np.ascontiguousarray(np.asarray(L0_attn_hidden_states, dtype=np.float32))
    B, Ss, D = x.shape
    assert (B, Ss, D) == (16, 2048, 1024)
    if _NC_CACHE is None:
        _NC_CACHE = build()
    nc = _NC_CACHE

    common = {
        "wq": np.ascontiguousarray(np.asarray(L0_attn_Wq, np.float32)),
        "wk": np.ascontiguousarray(np.asarray(L0_attn_Wk, np.float32)),
        "wv": np.ascontiguousarray(np.asarray(L0_attn_Wv, np.float32)),
        "bq": np.ascontiguousarray(np.asarray(L0_attn_bq, np.float32)),
        "bk": np.ascontiguousarray(np.asarray(L0_attn_bk, np.float32)),
        "bv": np.ascontiguousarray(np.asarray(L0_attn_bv, np.float32)),
        "wf": np.ascontiguousarray(np.asarray(L0_ffn_weight, np.float32)),
        "gam": np.ascontiguousarray(np.asarray(L0_ffn_gamma, np.float32)),
        "bet": np.ascontiguousarray(np.asarray(L0_ffn_beta, np.float32)),
        "id16": np.eye(128, dtype=ml_dtypes.bfloat16),
        "idf": np.eye(128, dtype=np.float32),
    }
    in_maps = []
    for core in range(8):
        xs = x[core * NB:(core + 1) * NB].reshape(NB * 2048, 1024)
        m = dict(common)
        m["x_in"] = np.ascontiguousarray(xs)
        in_maps.append(m)

    res = run_bass_kernel_spmd(nc, in_maps, core_ids=list(range(8)))
    out = np.empty((16, 2048, 1024), np.float32)
    for core in range(8):
        out[core * NB:(core + 1) * NB] = res.results[core]["out"].reshape(NB, 2048, 1024)
    return out
